# version 01 (final)
# baseline (speedup 1.0000x reference)
import numpy as np
import ml_dtypes
from contextlib import ExitStack

import concourse.bass as bass
import concourse.mybir as mybir
from concourse.bass_utils import run_bass_kernel_spmd

F32 = mybir.dt.float32
BF16 = mybir.dt.bfloat16
AF = mybir.ActivationFunctionType
ALU = mybir.AluOpType
AX = mybir.AxisListType
BF16NP = ml_dtypes.bfloat16

B = 2
T = 8192
D = 2048
Q = 2048
NCORE = 8
HEADS = 16
HD = 128
CH = 128
INTER = 4096
NST = 16
EPS_LN = 1e-5
EPS_ATTN = 1e-6

ENGINES = ("sync", "scalar", "tensor", "vector", "gpsimd")


class Prog:

    def __init__(self, nc):
        self.nc = nc
        self.ctx = ExitStack()
        self.sems = {e: self.ctx.enter_context(nc.semaphore(f"S_{e}")) for e in ENGINES}
        self.steps = []
        self._cur = None
        self.step()

    def step(self):
        self._cur = {e: [] for e in ENGINES}
        self.steps.append(self._cur)

    def do(self, eng, emit, inc=1, wait_self=False):
        self._cur[eng].append((emit, inc, wait_self))

    def dma(self, eng, out, in_, **kw):
        self.do(eng, (lambda E, o=out, i=in_, kw=kw: E.dma_start(out=o, in_=i, **kw)), inc=16)

    def sbuf(self, name, shape, dtype):
        return self.ctx.enter_context(self.nc.sbuf_tensor(name, list(shape), dtype))

    def psum(self, name, shape, dtype=F32):
        return self.ctx.enter_context(self.nc.psum_tensor(name, list(shape), dtype))

    def const_tile(self, name, parts, val, dtype=F32):
        t = self.sbuf(name, (parts, 1), dtype)
        self.do("vector", lambda E, t=t, val=val: E.memset(t[:], val))
        return t

    def init_negone(self):
        self.negone = self.const_tile("negone", 128, -1.0)

    def sbuf_cm(self, name, shape, dtype):
        cm = self.nc.sbuf_tensor(name, list(shape), dtype)
        return cm.__enter__(), cm

    def psum_cm(self, name, shape, dtype=F32):
        cm = self.nc.psum_tensor(name, list(shape), dtype)
        return cm.__enter__(), cm

    @staticmethod
    def free(cm):
        cm.__exit__(None, None, None)

    def build(self):
        nsteps = len(self.steps)
        cum = {e: [0] for e in ENGINES}
        for st in self.steps:
            for e in ENGINES:
                cum[e].append(cum[e][-1] + sum(inc for _, inc, _ in st[e]))
        used = [e for e in ENGINES if cum[e][-1] > 0]

        with self.nc.Block() as block:
            for e in used:
                def body(E, e=e):
                    last_wait = {o: 0 for o in used}
                    own = 0
                    for si in range(nsteps):
                        ops = self.steps[si][e]
                        if not ops:
                            continue
                        for o in used:
                            if o != e and cum[o][si] > last_wait[o]:
                                E.wait_ge(self.sems[o], cum[o][si])
                                last_wait[o] = cum[o][si]
                        auto_ws = e in ("vector", "scalar", "gpsimd")
                        for emit, inc, ws in ops:
                            if (ws or (auto_ws and inc == 1)) and own > 0:
                                E.wait_ge(self.sems[e], own)
                            emit(E).then_inc(self.sems[e], inc)
                            own += inc
                    if own > 0:
                        E.wait_ge(self.sems[e], own)
                getattr(block, e)(body)
        self.ctx.close()
        return self.nc


def bcast_rows_via_dram(p, scratch_dram, row_aps, bufs, width):
    p.step()
    for i, ra in enumerate(row_aps):
        p.dma("sync", scratch_dram[i:i + 1, 0:width], ra)
    p.step()
    for i, bt in enumerate(bufs):
        src = bass.AP(scratch_dram.tensor, i * scratch_dram.shape[1], [[0, 128], [1, width]])
        p.dma("scalar", bt[:, 0:width], src)


def emit_ln_stats(p, nc, tiles_f32, ntiles, width, psum_s, psum_q, ones_col, sqbufs):
    nN = (width + 511) // 512
    for i in range(ntiles):
        sq = sqbufs[i % 2]
        p.step()
        p.do("scalar", lambda E, i=i, sq=sq: E.activation(sq[:, 0:width], tiles_f32(i), AF.Square))
        p.step()
        for j in range(nN):
            n0, n1 = j * 512, min(width, (j + 1) * 512)
            p.do("tensor", lambda E, i=i, n0=n0, n1=n1: E.matmul(
                psum_s[0:1, n0:n1], ones_col[:, 0:1], tiles_f32(i)[:, n0:n1],
                start=(i == 0), stop=(i == ntiles - 1)))
            p.do("tensor", lambda E, i=i, sq=sq, n0=n0, n1=n1: E.matmul(
                psum_q[0:1, n0:n1], ones_col[:, 0:1], sq[:, n0:n1],
                start=(i == 0), stop=(i == ntiles - 1)))


def emit_ln_finish(p, nc, psum_s, psum_q, width, d_count, rows, eps_col):
    w = slice(0, width)
    inv = 1.0 / d_count
    mean, tmp, rstd = rows["mean"], rows["tmp"], rows["rstd"]
    p.step()
    p.do("vector", lambda E: E.tensor_scalar_mul(mean[0:1, w], psum_s[0:1, w], inv))
    p.do("vector", lambda E: E.tensor_scalar_mul(tmp[0:1, w], psum_q[0:1, w], inv))
    p.step()
    p.do("vector", lambda E: E.tensor_tensor(rstd[0:1, w], mean[0:1, w], mean[0:1, w], ALU.mult))
    p.step()
    p.do("vector", lambda E: E.tensor_tensor(tmp[0:1, w], tmp[0:1, w], rstd[0:1, w], ALU.subtract))
    p.step()
    p.do("scalar", lambda E: E.activation(tmp[0:1, w], tmp[0:1, w], AF.Sqrt, bias=eps_col[0:1, 0:1]))
    p.step()
    p.do("scalar", lambda E: E.activation(tmp[0:1, w], tmp[0:1, w], AF.Ln))
    p.step()
    p.do("scalar", lambda E: E.activation(rstd[0:1, w], tmp[0:1, w], AF.Exp, scale=p.negone[0:1, 0:1]))
    p.step()
    p.do("vector", lambda E: E.tensor_tensor(tmp[0:1, w], mean[0:1, w], rstd[0:1, w], ALU.mult))
    p.step()
    p.do("vector", lambda E: E.tensor_scalar_mul(tmp[0:1, w], tmp[0:1, w], -1.0))


def make_ln_rows(p, prefix, width):
    r = {n: p.sbuf(f"{prefix}_{n}", (1, width), F32) for n in ("mean", "tmp", "rstd")}
    r["shift"] = r["tmp"]
    return r



def build_a0():
    nc = bass.Bass("TRN2", target_bir_lowering=False)
    xq = nc.dram_tensor("xq", [D, Q], F32, kind="ExternalInput").ap()
    lnq = nc.dram_tensor("lnq", [D, Q], BF16, kind="ExternalOutput").ap()
    rowscr = nc.dram_tensor("rowscr", [2, Q], F32, kind="Internal").ap()

    p = Prog(nc)
    p.init_negone()
    nt = D // 128
    xt = [p.sbuf(f"x{i}", (128, Q), F32) for i in range(2)]
    sqb = [p.sbuf(f"sq{i}", (128, Q), F32) for i in range(2)]
    ones = p.sbuf("ones", (128, 1), F32)
    epsc = p.const_tile("epsc", 1, EPS_LN)
    rows = make_ln_rows(p, "ln", Q)
    rstd_b = p.sbuf("rstd_b", (128, Q), F32)
    shift_b = p.sbuf("shift_b", (128, Q), F32)
    lnt = p.sbuf("lnt", (128, Q), BF16)
    tmp = p.sbuf("tmp", (128, Q), F32)
    ps_s = p.psum("ps_s", (1, Q))
    ps_q = p.psum("ps_q", (1, Q))

    p.do("vector", lambda E: E.memset(ones[:], 1.0))
    p.dma("sync", xt[0][:], xq[0:128, :])
    for i in range(nt):
        cur, sq = xt[i % 2], sqb[i % 2]
        p.step()
        if i + 1 < nt:
            p.dma("sync", xt[(i + 1) % 2][:], xq[(i + 1) * 128:(i + 2) * 128, :])
        p.do("scalar", lambda E, cur=cur, sq=sq: E.activation(sq[:], cur[:], AF.Square))
        for j in range(Q // 512):
            n0 = j * 512
            p.do("tensor", lambda E, cur=cur, n0=n0, i=i: E.matmul(
                ps_s[0:1, n0:n0 + 512], ones[:, 0:1], cur[:, n0:n0 + 512],
                start=(i == 0), stop=(i == nt - 1)))
        p.step()
        for j in range(Q // 512):
            n0 = j * 512
            p.do("tensor", lambda E, sq=sq, n0=n0, i=i: E.matmul(
                ps_q[0:1, n0:n0 + 512], ones[:, 0:1], sq[:, n0:n0 + 512],
                start=(i == 0), stop=(i == nt - 1)))
    emit_ln_finish(p, nc, ps_s, ps_q, Q, D, rows, epsc)
    bcast_rows_via_dram(p, rowscr, [rows["rstd"][0:1, :], rows["shift"][0:1, :]], [rstd_b, shift_b], Q)
    p.step()
    p.dma("sync", xt[0][:], xq[0:128, :])
    for i in range(nt):
        cur = xt[i % 2]
        p.step()
        if i + 1 < nt:
            p.dma("sync", xt[(i + 1) % 2][:], xq[(i + 1) * 128:(i + 2) * 128, :])
        p.do("vector", lambda E, cur=cur: E.tensor_tensor(tmp[:], cur[:], rstd_b[:], ALU.mult))
        p.step()
        p.do("vector", lambda E: E.tensor_tensor(lnt[:], tmp[:], shift_b[:], ALU.add))
        p.step()
        p.dma("gpsimd", lnq[i * 128:(i + 1) * 128, :], lnt[:])
    p.step()
    return p.build()



def build_a(debug=False):
    nc = bass.Bass("TRN2", target_bir_lowering=False)
    ln1 = nc.dram_tensor("ln1", [D, T], BF16, kind="ExternalInput").ap()
    w_in = {n: nc.dram_tensor(n, [D, 4 * HD], F32, kind="ExternalInput").ap()
            for n in ("wq_s", "wk_s", "wv_s")}
    b_in = {n: nc.dram_tensor(n, [1, 4 * HD], F32, kind="ExternalInput").ap()
            for n in ("bq_s", "bk_s", "bv_s")}
    maskT_d = nc.dram_tensor("maskT", [CH, CH], F32, kind="ExternalInput").ap()
    ident_d = nc.dram_tensor("ident", [128, 128], F32, kind="ExternalInput").ap()
    attn_o = nc.dram_tensor("attn_s", [4 * HD, T], BF16, kind="ExternalOutput").ap()
    if debug:
        dbg_q = nc.dram_tensor("dbg_q", [128, T], BF16, kind="ExternalOutput").ap()
        dbg_k = nc.dram_tensor("dbg_k", [128, T], BF16, kind="ExternalOutput").ap()
        dbg_v = nc.dram_tensor("dbg_v", [128, T], BF16, kind="ExternalOutput").ap()
        dbg_st = nc.dram_tensor("dbg_st", [128, CH + 1], F32, kind="ExternalOutput").ap()
        dbg_o = nc.dram_tensor("dbg_o", [128, T], BF16, kind="ExternalOutput").ap()
        dbg_dn = nc.dram_tensor("dbg_dn", [128, 4], F32, kind="ExternalOutput").ap()
        dbg_tm = nc.dram_tensor("dbg_tm", [128, 128], BF16, kind="ExternalOutput").ap()
        dbg_s1 = nc.dram_tensor("dbg_s1", [128, CH + 1], F32, kind="ExternalOutput").ap()

    p = Prog(nc)
    p.init_negone()
    KT = D // 128
    NT = T // 512
    wsb = {n: p.sbuf(n + "_b", (128, KT * 512), BF16) for n in ("wq_s", "wk_s", "wv_s")}
    bcol = {n: p.sbuf(n + "_c", (128, 4), F32) for n in ("bq_s", "bk_s", "bv_s")}
    maskT = p.sbuf("maskT_b", (CH, CH), BF16)
    lnblk = [p.sbuf(f"lnblk{i}", (128, KT * 512), BF16) for i in range(2)]
    qf = p.sbuf("qf", (128, T), BF16)
    kf = p.sbuf("kf", (128, T), BF16)
    vf = p.sbuf("vf", (128, T), BF16)
    attn_fm = p.sbuf("attn_fm", (128, T), BF16)
    mb = p.sbuf("mb", (128, 512), F32)
    rb = p.sbuf("rb", (128, 512), F32)
    eb = p.sbuf("eb", (128, 512), F32)
    mb2 = p.sbuf("mb2", (128, 512), F32)
    rb2 = p.sbuf("rb2", (128, 512), F32)
    eb2 = p.sbuf("eb2", (128, 512), F32)
    kf_tm = p.sbuf("kf_tm", (CH, CH), BF16)
    vca = p.sbuf("vca", (CH, CH + 1), BF16)
    a_m = p.sbuf("a_m", (CH, CH), BF16)
    state = p.sbuf("state", (128, CH + 1), F32)
    state_bf = p.sbuf("state_bf", (128, CH + 1), BF16)
    attn_tm = p.sbuf("attn_tm", (CH, CH), BF16)
    dn = p.sbuf("dn", (CH, 4), F32)
    negc = p.const_tile("negc", 128, -1.0)
    dn3 = p.sbuf("dn3", (CH, CH + 1), F32)
    ident = p.sbuf("ident_b", (128, 128), BF16)

    ps_q = p.psum("ps_q", (128, 512))
    ps_k = p.psum("ps_k", (128, 512))
    ps_v = p.psum("ps_v", (128, 512))
    ps_tr = p.psum("ps_tr", (128, 256), BF16)
    ps_a = p.psum("ps_a", (CH, CH))
    ps_s = p.psum("ps_s", (CH, CH + 1))
    ps_kv = p.psum("ps_kv", (128, CH + 1))
    ps_tro = p.psum("ps_tro", (128, 128), BF16)

    for wn, bn in (("wq_s", "bq_s"), ("wk_s", "bk_s"), ("wv_s", "bv_s")):
        wsrc = bass.AP(w_in[wn].tensor, 0, [[512, 128], [512 * 128, KT], [1, 512]])
        p.dma("gpsimd", wsb[wn][:].rearrange("p (k n) -> p k n", k=KT), wsrc)
        bsrc = bass.AP(b_in[bn].tensor, 0, [[1, 128], [128, 4]])
        p.dma("gpsimd", bcol[bn][:], bsrc, allow_slow_non_contiguous=True)
    p.dma("gpsimd", maskT[:], maskT_d[:])
    p.dma("gpsimd", ident[:], ident_d[:])
    p.step()

    for hh in range(4):
        p.step()
        p.dma("sync", lnblk[0][:].rearrange("p (k n) -> p k n", k=KT),
              bass.AP(ln1.tensor, 0, [[T, 128], [T * 128, KT], [1, 512]]))
        for tn in range(NT):
            t0 = tn * 512
            cur = lnblk[tn % 2]
            p.step()
            if tn + 1 < NT:
                nxt = lnblk[(tn + 1) % 2]
                p.dma("sync", nxt[:].rearrange("p (k n) -> p k n", k=KT),
                      bass.AP(ln1.tensor, (tn + 1) * 512, [[T, 128], [T * 128, KT], [1, 512]]))
            for ps, wn in ((ps_q, "wq_s"), (ps_k, "wk_s"), (ps_v, "wv_s")):
                for k in range(KT):
                    p.do("tensor", lambda E, ps=ps, wn=wn, k=k, cur=cur, hh=hh: E.matmul(
                        ps[:, 0:512], wsb[wn][:, k * 512 + hh * 128: k * 512 + hh * 128 + 128],
                        cur[:, k * 512:(k + 1) * 512], start=(k == 0), stop=(k == KT - 1)))
            p.step()
            p.do("vector", lambda E, hh=hh: E.tensor_scalar(mb[:], ps_q[:, 0:512], bcol["bq_s"][:, hh:hh + 1], 0.0, ALU.add, ALU.min))
            p.do("vector", lambda E, hh=hh: E.tensor_scalar(rb[:], ps_q[:, 0:512], bcol["bq_s"][:, hh:hh + 1], 0.0, ALU.add, ALU.max))
            p.do("vector", lambda E, hh=hh: E.tensor_scalar(mb2[:], ps_k[:, 0:512], bcol["bk_s"][:, hh:hh + 1], 0.0, ALU.add, ALU.min))
            p.do("vector", lambda E, hh=hh: E.tensor_scalar(rb2[:], ps_k[:, 0:512], bcol["bk_s"][:, hh:hh + 1], 0.0, ALU.add, ALU.max))
            p.do("vector", lambda E, t0=t0, hh=hh: E.tensor_scalar(vf[:, t0:t0 + 512], ps_v[:, 0:512], bcol["bv_s"][:, hh:hh + 1], None, ALU.add))
            p.step()
            p.do("scalar", lambda E: E.activation(eb[:], mb[:], AF.Exp))
            p.do("scalar", lambda E: E.activation(eb2[:], mb2[:], AF.Exp))
            p.step()
            p.do("vector", lambda E, t0=t0: E.tensor_tensor(qf[:, t0:t0 + 512], eb[:], rb[:], ALU.add))
            p.do("vector", lambda E, t0=t0: E.tensor_tensor(kf[:, t0:t0 + 512], eb2[:], rb2[:], ALU.add))
        if debug and hh == 0:
            p.step()
            p.dma("sync", dbg_q[:], qf[:])
            p.dma("sync", dbg_k[:], kf[:])
            p.dma("sync", dbg_v[:], vf[:])
        p.step()
        p.do("vector", lambda E: E.memset(state[:], 0.0))
        p.do("vector", lambda E: E.memset(state_bf[:], 0.0))
        p.do("vector", lambda E: E.memset(vca[:, CH:CH + 1], 1.0))
        p.step()
        for j in range(T // CH):
            c0 = j * CH
            p.step()
            p.do("tensor", lambda E, c0=c0: E.transpose(ps_tr[:, 0:128], kf[:, c0:c0 + CH], ident[:]))
            p.do("tensor", lambda E, c0=c0: E.transpose(ps_tr[:, 128:256], vf[:, c0:c0 + CH], ident[:]))
            p.do("tensor", lambda E, c0=c0: E.matmul(ps_a[:], kf[:, c0:c0 + CH], qf[:, c0:c0 + CH], start=True, stop=True))
            p.do("tensor", lambda E, c0=c0: E.matmul(ps_s[:], qf[:, c0:c0 + CH], state_bf[:], start=True, stop=False))
            p.step()
            p.do("vector", lambda E: E.tensor_copy(kf_tm[:], ps_tr[:, 0:128]))
            p.do("vector", lambda E: E.tensor_copy(vca[:, 0:CH], ps_tr[:, 128:256]))
            p.do("vector", lambda E: E.tensor_tensor(a_m[:], ps_a[:], maskT[:], ALU.mult))
            p.step()
            p.do("tensor", lambda E: E.matmul(ps_kv[:], kf_tm[:], vca[:], start=True, stop=True))
            p.do("tensor", lambda E: E.matmul(ps_s[:], a_m[:], vca[:], start=False, stop=True))
            p.step()
            p.do("vector", lambda E: E.tensor_tensor(state[:], state[:], ps_kv[:], ALU.add))
            p.do("vector", lambda E: E.tensor_scalar_add(dn[:, 0:1], ps_s[:, CH:CH + 1], EPS_ATTN))
            p.step()
            p.do("scalar", lambda E: E.activation(dn[:, 2:3], dn[:, 0:1], AF.Ln))
            p.do("vector", lambda E: E.tensor_copy(state_bf[:], state[:]))
            p.step()
            p.do("scalar", lambda E: E.activation(dn[:, 1:2], dn[:, 2:3], AF.Exp, scale=p.negone[:, 0:1]))
            p.step()
            p.do("vector", lambda E: E.tensor_scalar(attn_tm[:], ps_s[:, 0:CH], dn[:, 1:2], None, ALU.mult), wait_self=True)
            p.step()
            p.do("tensor", lambda E: E.transpose(ps_tro[:], attn_tm[:], ident[:]))
            p.step()
            p.do("vector", lambda E, c0=c0: E.tensor_copy(attn_fm[:, c0:c0 + CH], ps_tro[:]))
            if debug and hh == 0 and j == 0:
                p.step()
                p.dma("sync", dbg_dn[:, 0:2], dn[:, 0:2])
                p.dma("sync", dbg_tm[:], attn_tm[:])
            if debug and hh == 0 and j == 1:
                p.step()
                p.dma("sync", dbg_st[:], state[:])
                p.dma("sync", dbg_dn[:, 2:4], dn[:, 0:2])
                p.do("vector", lambda E: E.tensor_copy(dn3[:], ps_s[:]))
                p.step()
                p.dma("sync", dbg_s1[:], dn3[:])
        p.step()
        p.dma("sync", attn_o[hh * 128:(hh + 1) * 128, :], attn_fm[:])
        if debug and hh == 0:
            p.dma("sync", dbg_o[:], attn_fm[:])
    p.step()
    return p.build()



def build_b(debug=False):
    nc = bass.Bass("TRN2", target_bir_lowering=False)
    xq = nc.dram_tensor("xq", [D, Q], F32, kind="ExternalInput").ap()
    attn_q = nc.dram_tensor("attn_q", [D, Q], BF16, kind="ExternalInput").ap()
    wo_d = nc.dram_tensor("wo", [D, D], F32, kind="ExternalInput").ap()
    bo_d = nc.dram_tensor("bo", [1, D], F32, kind="ExternalInput").ap()
    w0t_d = nc.dram_tensor("w0t", [D, D], F32, kind="ExternalInput").ap()
    rw0t_d = nc.dram_tensor("rw0t", [D, 8], F32, kind="ExternalInput").ap()
    aallt_d = nc.dram_tensor("a_allt", [D, 128], F32, kind="ExternalInput").ap()
    bstk_d = nc.dram_tensor("b_stack", [128, D], F32, kind="ExternalInput").ap()
    ident_d = nc.dram_tensor("ident", [128, 128], F32, kind="ExternalInput").ap()
    out2_o = nc.dram_tensor("out2q", [D, Q], F32, kind="ExternalOutput").ap()
    ln3_o = nc.dram_tensor("ln3q", [D, Q], BF16, kind="ExternalOutput").ap()
    x1scr = nc.dram_tensor("x1scr", [D, Q], F32, kind="Internal").ap()
    rowscr = nc.dram_tensor("rowscr", [4, Q], F32, kind="Internal").ap()
    if debug:
        dbg_x1 = nc.dram_tensor("dbg_x1", [D, Q], F32, kind="ExternalOutput").ap()
        dbg_raw = nc.dram_tensor("dbg_raw", [8, Q], F32, kind="ExternalOutput").ap()
        dbg_wtok = nc.dram_tensor("dbg_wtok", [128, 16 * 8], F32, kind="ExternalOutput").ap()
        dbg_ln2 = nc.dram_tensor("dbg_ln2", [128, Q], BF16, kind="ExternalOutput").ap()
        dbg_stk = nc.dram_tensor("dbg_stk", [128, Q], BF16, kind="ExternalOutput").ap()
        dbg_rows = nc.dram_tensor("dbg_rows", [4, Q], F32, kind="ExternalOutput").ap()
        dbg_bc = nc.dram_tensor("dbg_bc", [256, Q], F32, kind="ExternalOutput").ap()
        dbg_bc2 = nc.dram_tensor("dbg_bc2", [256, Q], F32, kind="ExternalOutput").ap()

    p = Prog(nc)
    p.init_negone()
    KT = D // 128
    big = p.sbuf("big", (128, KT * Q), BF16)
    wt = [p.sbuf(f"wt{i}", (128, KT * 128), BF16) for i in range(2)]
    ident = p.sbuf("ident_b", (128, 128), BF16)
    identf8 = p.sbuf("identf8", (8, 8), F32)
    aallb = p.sbuf("aallb", (128, KT * 128), BF16)
    bstk = p.sbuf("bstk", (128, D), BF16)
    stacked = p.sbuf("stacked", (128, Q), BF16)
    axs = p.sbuf("axs", (128, 128), BF16)
    wtok = p.sbuf("wtok", (128, KT * 8), F32)
    rwsb = p.sbuf("rwsb", (128, 8), F32)
    onescol = p.sbuf("onescol", (128, 1), F32)
    epsc = p.const_tile("epsc", 1, EPS_LN)
    rows = make_ln_rows(p, "ln", Q)
    accs = p.sbuf("accs", (1, Q), F32)
    accq = p.sbuf("accq", (1, Q), F32)
    rawacc = p.sbuf("rawacc", (8, Q), F32)
    sqb = [p.sbuf(f"sq{i}", (128, Q), F32) for i in range(2)]
    xt = p.sbuf("xt", (128, Q), F32)
    x1t = p.sbuf("x1t", (128, Q), F32)
    rw0t_f = p.sbuf("rw0t_f", (128, KT * 8), F32)
    bocol = p.sbuf("bocol", (128, KT), F32)
    scol = p.sbuf("scol", (128, KT), F32)
    shcol = p.sbuf("shcol", (128, KT), F32)
    lgb = p.sbuf("lgb", (128, 8), F32)
    eqb = p.sbuf("eqb", (128, 8), F32)
    ebx = p.sbuf("ebx", (128, 8), F32)
    m12 = p.sbuf("m12", (128, 8), F32)
    lnstg = p.sbuf("lnstg", (128, Q), BF16)
    ot = p.sbuf("ot", (128, Q), F32)
    ot2dbg = p.sbuf("ot2dbg", (128, Q), F32)
    ps_mm = p.psum("ps_mm", (128, 512))
    ps_st = p.psum("ps_st", (8, Q))
    ps_misc = p.psum("ps_misc", (128, 128))
    ps_tr = p.psum("ps_tr", (128, 128), BF16)

    def wslice_src(dram, o):
        return bass.AP(dram.tensor, o * 128, [[D, 128], [D * 128, KT], [1, 128]])

    p.do("vector", lambda E: E.memset(onescol[:], 1.0))
    p.do("vector", lambda E: E.memset(accs[:], 0.0))
    p.do("vector", lambda E: E.memset(accq[:], 0.0))
    p.do("vector", lambda E: E.memset(rawacc[:], 0.0))
    p.dma("gpsimd", big[:].rearrange("p (k t) -> p k t", k=KT),
          bass.AP(attn_q.tensor, 0, [[Q, 128], [Q * 128, KT], [1, Q]]))
    p.dma("gpsimd", ident[:], ident_d[:])
    p.dma("sync", identf8[:], ident_d[0:8, 0:8])
    p.dma("gpsimd", aallb[:].rearrange("p (k c) -> p k c", k=KT),
          bass.AP(aallt_d.tensor, 0, [[128, 128], [128 * 128, KT], [1, 128]]))
    p.dma("gpsimd", bstk[:], bstk_d[:])
    p.dma("gpsimd", bocol[:], bass.AP(bo_d.tensor, 0, [[1, 128], [128, KT]]),
          allow_slow_non_contiguous=True)
    p.dma("sync", rw0t_f[:].rearrange("p (k c) -> p k c", k=KT),
          bass.AP(rw0t_d.tensor, 0, [[8, 128], [8 * 128, KT], [1, 8]]))
    p.dma("sync", xt[:], xq[0:128, :])
    p.dma("gpsimd", wt[0][:].rearrange("p (k c) -> p k c", k=KT), wslice_src(wo_d, 0))
    p.step()
    for k in range(KT):
        p.do("tensor", lambda E, k=k: E.matmul(ps_misc[0:8, 0:1], rw0t_f[:, k * 8:(k + 1) * 8],
                                               onescol[:, 0:1], start=(k == 0), stop=(k == KT - 1)))
    p.step()
    p.do("vector", lambda E: E.tensor_copy(m12[0:8, 0:1], ps_misc[0:8, 0:1]))

    def mm_residual_phase(w_dram, res_tile_fn, extra_mm, out_write, first_o_w):
        pass

    for o in range(KT):
        for tn in range(Q // 512):
            p.step()
            if tn == 0 and o + 1 < KT:
                p.dma("gpsimd", wt[(o + 1) % 2][:].rearrange("p (k c) -> p k c", k=KT),
                      wslice_src(wo_d, o + 1))
            for k in range(KT):
                p.do("tensor", lambda E, o=o, tn=tn, k=k: E.matmul(
                    ps_mm[:], wt[o % 2][:, k * 128:(k + 1) * 128],
                    big[:, k * Q + tn * 512: k * Q + tn * 512 + 512],
                    start=(k == 0), stop=(k == KT - 1)))
            p.step()
            p.do("vector", lambda E, o=o, tn=tn: E.scalar_tensor_tensor(
                x1t[:, tn * 512:(tn + 1) * 512], ps_mm[:], bocol[:, o:o + 1],
                xt[:, tn * 512:(tn + 1) * 512], ALU.add, ALU.add))
        p.step()
        p.do("scalar", lambda E: E.activation(sqb[0][:], x1t[:], AF.Square))
        p.dma("sync", x1scr[o * 128:(o + 1) * 128, :], x1t[:])
        for j in range(Q // 512):
            p.do("tensor", lambda E, n0=j * 512: E.matmul(
                ps_st[0:1, n0:n0 + 512], onescol[:, 0:1], x1t[:, n0:n0 + 512],
                start=True, stop=True))
        p.step()
        if o + 1 < KT:
            p.dma("sync", xt[:], xq[(o + 1) * 128:(o + 2) * 128, :])
        p.do("vector", lambda E: E.tensor_tensor(accs[0:1, :], accs[0:1, :], ps_st[0:1, :], ALU.add))
        p.step()
        for j in range(Q // 512):
            p.do("tensor", lambda E, n0=j * 512: E.matmul(
                ps_st[0:1, n0:n0 + 512], onescol[:, 0:1], sqb[0][:, n0:n0 + 512],
                start=True, stop=True))
        p.step()
        p.do("vector", lambda E: E.tensor_tensor(accq[0:1, :], accq[0:1, :], ps_st[0:1, :], ALU.add))
        p.step()
        for j in range(Q // 512):
            p.do("tensor", lambda E, o=o, n0=j * 512: E.matmul(
                ps_st[0:8, n0:n0 + 512], rw0t_f[:, o * 8:(o + 1) * 8],
                x1t[:, n0:n0 + 512], start=True, stop=True))
        p.step()
        p.do("vector", lambda E: E.tensor_tensor(rawacc[:], rawacc[:], ps_st[0:8, :], ALU.add))
    emit_ln_finish(p, nc, accs, accq, Q, D, rows, epsc)
    p.step()
    p.dma("sync", rowscr[0:1, :], rows["rstd"][0:1, :])
    p.dma("sync", rowscr[1:2, :], rows["shift"][0:1, :])
    p.dma("sync", rowscr[3:4, 0:8], m12[0:8, 0:1], allow_slow_non_contiguous=True)
    p.step()
    p.dma("gpsimd", sqb[0][:], bass.AP(rowscr.tensor, 0, [[0, 128], [1, Q]]))
    p.dma("gpsimd", sqb[1][:], bass.AP(rowscr.tensor, Q, [[0, 128], [1, Q]]))
    p.dma("gpsimd", scol[:], bass.AP(rowscr.tensor, 0, [[1, 128], [128, KT]]), allow_slow_non_contiguous=True)
    p.dma("gpsimd", shcol[:], bass.AP(rowscr.tensor, Q, [[1, 128], [128, KT]]), allow_slow_non_contiguous=True)
    p.dma("gpsimd", rwsb[:], bass.AP(rowscr.tensor, 3 * Q, [[0, 128], [1, 8]]))
    for k in range(KT):
        p.step()
        p.dma("gpsimd", x1t[:], x1scr[k * 128:(k + 1) * 128, :])
        p.step()
        if debug and k == 0:
            p.dma("sync", dbg_bc2[0:128, :], sqb[0][:])
        p.do("vector", lambda E: E.tensor_tensor(ot[:], x1t[:], sqb[0][:], ALU.mult))
        if debug and k == 0:
            p.do("gpsimd", lambda E: E.tensor_tensor(lnstg[:].bitcast(F32) if False else ot2dbg[:], x1t[:], sqb[0][:], ALU.mult))
        p.step()
        p.do("vector", lambda E, k=k: E.tensor_tensor(big[:, k * Q:(k + 1) * Q], ot[:], sqb[1][:], ALU.add))
        if debug and k == 0:
            p.step()
            p.dma("sync", dbg_bc[0:128, :], x1t[:])
            p.dma("sync", dbg_bc[128:256, :], ot[:])
            p.dma("sync", dbg_bc2[128:256, :], ot2dbg[:])
    for tt in range(KT):
        t0 = tt * 128
        p.step()
        p.do("tensor", lambda E, t0=t0: E.matmul(ps_misc[0:128, 0:8], rawacc[0:8, t0:t0 + 128],
                                                 identf8[:], start=True, stop=True))
        p.step()
        p.do("vector", lambda E, tt=tt: E.tensor_scalar(lgb[:], ps_misc[0:128, 0:8], scol[:, tt:tt + 1], None, ALU.mult))
        p.step()
        p.do("vector", lambda E, tt=tt: E.scalar_tensor_tensor(lgb[:], rwsb[:], shcol[:, tt:tt + 1], lgb[:], ALU.mult, ALU.add))
        p.step()
        p.do("vector", lambda E: E.reduce_max(m12[:, 0:1], lgb[:], AX.X))
        p.step()
        p.do("vector", lambda E: E.tensor_scalar(eqb[:], lgb[:], m12[:, 0:1], None, ALU.is_equal), wait_self=True)
        p.step()
        p.do("vector", lambda E: E.scalar_tensor_tensor(eqb[:], eqb[:], -1e30, lgb[:], ALU.mult, ALU.add))
        p.step()
        p.do("vector", lambda E: E.reduce_max(m12[:, 1:2], eqb[:], AX.X))
        p.step()
        p.do("vector", lambda E: E.tensor_scalar_mul(m12[:, 2:3], m12[:, 0:1], -1.0), wait_self=True)
        p.do("vector", lambda E: E.tensor_tensor(m12[:, 3:4], m12[:, 1:2], m12[:, 0:1], ALU.subtract))
        p.step()
        p.do("scalar", lambda E: E.activation(ebx[:], lgb[:], AF.Exp, bias=m12[:, 2:3]))
        p.do("scalar", lambda E: E.activation(m12[:, 4:5], m12[:, 3:4], AF.Exp))
        p.step()
        p.do("vector", lambda E: E.tensor_scalar_add(m12[:, 4:5], m12[:, 4:5], 1.0))
        p.step()
        p.do("scalar", lambda E: E.activation(m12[:, 5:6], m12[:, 4:5], AF.Ln))
        p.step()
        p.do("scalar", lambda E: E.activation(m12[:, 6:7], m12[:, 5:6], AF.Exp, scale=p.negone[:, 0:1]))
        p.step()
        p.do("vector", lambda E: E.tensor_scalar(eqb[:], lgb[:], m12[:, 1:2], None, ALU.is_ge), wait_self=True)
        p.step()
        p.do("vector", lambda E: E.tensor_tensor(ebx[:], ebx[:], eqb[:], ALU.mult))
        p.step()
        p.do("vector", lambda E, tt=tt: E.tensor_scalar(wtok[:, tt * 8:(tt + 1) * 8], ebx[:], m12[:, 6:7], 2.0, ALU.mult, ALU.mult), wait_self=True)
    for tt in range(KT):
        t0 = tt * 128
        p.step()
        for k in range(KT):
            p.do("tensor", lambda E, tt=tt, k=k: E.matmul(
                ps_misc[:], big[:, k * Q + tt * 128: k * Q + tt * 128 + 128],
                aallb[:, k * 128:(k + 1) * 128], start=(k == 0), stop=(k == KT - 1)))
        p.step()
        for e in range(8):
            p.do("vector", lambda E, tt=tt, e=e: E.tensor_scalar(
                axs[:, e * 16:(e + 1) * 16], ps_misc[:, e * 16:(e + 1) * 16],
                wtok[:, tt * 8 + e: tt * 8 + e + 1], None, ALU.mult))
        p.step()
        p.do("tensor", lambda E: E.transpose(ps_tr[:], axs[:], ident[:]))
        p.step()
        p.do("vector", lambda E, t0=t0: E.tensor_copy(stacked[:, t0:t0 + 128], ps_tr[:]))
    if debug:
        p.step()
        p.dma("sync", dbg_rows[0:2, :], rowscr[0:2, :])
        p.dma("sync", dbg_rows[2:3, :], accs[0:1, :])
        p.dma("sync", dbg_rows[3:4, :], accq[0:1, :])
        p.dma("sync", dbg_raw[:], rawacc[:])
        p.dma("sync", dbg_wtok[:], wtok[:])
        p.dma("sync", dbg_ln2[:], big[:, 0:Q])
        p.dma("sync", dbg_stk[:], stacked[:])
        for k in range(KT):
            p.dma("gpsimd", dbg_x1[k * 128:(k + 1) * 128, :], x1scr[k * 128:(k + 1) * 128, :])
    p.step()
    p.do("vector", lambda E: E.memset(accs[:], 0.0))
    p.do("vector", lambda E: E.memset(accq[:], 0.0))
    p.dma("sync", x1t[:], x1scr[0:128, :])
    p.dma("gpsimd", wt[0][:].rearrange("p (k c) -> p k c", k=KT), wslice_src(w0t_d, 0))
    for o in range(KT):
        for tn in range(Q // 512):
            p.step()
            if tn == 0 and o + 1 < KT:
                p.dma("gpsimd", wt[(o + 1) % 2][:].rearrange("p (k c) -> p k c", k=KT),
                      wslice_src(w0t_d, o + 1))
            for k in range(KT):
                p.do("tensor", lambda E, o=o, tn=tn, k=k: E.matmul(
                    ps_mm[:], wt[o % 2][:, k * 128:(k + 1) * 128],
                    big[:, k * Q + tn * 512: k * Q + tn * 512 + 512],
                    start=(k == 0), stop=False))
            p.do("tensor", lambda E, o=o, tn=tn: E.matmul(
                ps_mm[:], bstk[:, o * 128:(o + 1) * 128],
                stacked[:, tn * 512:(tn + 1) * 512], start=False, stop=True))
            p.step()
            p.do("vector", lambda E, tn=tn: E.tensor_tensor(
                xt[:, tn * 512:(tn + 1) * 512], ps_mm[:], x1t[:, tn * 512:(tn + 1) * 512], ALU.add))
        p.step()
        p.do("scalar", lambda E: E.activation(sqb[0][:], xt[:], AF.Square))
        p.dma("sync", out2_o[o * 128:(o + 1) * 128, :], xt[:])
        for j in range(Q // 512):
            p.do("tensor", lambda E, n0=j * 512: E.matmul(
                ps_st[0:1, n0:n0 + 512], onescol[:, 0:1], xt[:, n0:n0 + 512],
                start=True, stop=True))
        p.step()
        if o + 1 < KT:
            p.dma("sync", x1t[:], x1scr[(o + 1) * 128:(o + 2) * 128, :])
        p.do("vector", lambda E: E.tensor_tensor(accs[0:1, :], accs[0:1, :], ps_st[0:1, :], ALU.add))
        p.step()
        for j in range(Q // 512):
            p.do("tensor", lambda E, n0=j * 512: E.matmul(
                ps_st[0:1, n0:n0 + 512], onescol[:, 0:1], sqb[0][:, n0:n0 + 512],
                start=True, stop=True))
        p.step()
        p.do("vector", lambda E: E.tensor_tensor(accq[0:1, :], accq[0:1, :], ps_st[0:1, :], ALU.add))
    emit_ln_finish(p, nc, accs, accq, Q, D, rows, epsc)
    p.step()
    p.dma("sync", rowscr[0:1, :], rows["rstd"][0:1, :])
    p.dma("sync", rowscr[1:2, :], rows["shift"][0:1, :])
    p.step()
    p.dma("gpsimd", sqb[0][:], bass.AP(rowscr.tensor, 0, [[0, 128], [1, Q]]))
    p.dma("gpsimd", sqb[1][:], bass.AP(rowscr.tensor, Q, [[0, 128], [1, Q]]))
    for k in range(KT):
        p.step()
        p.dma("gpsimd", x1t[:], out2_o[k * 128:(k + 1) * 128, :])
        p.step()
        p.do("vector", lambda E: E.tensor_tensor(x1t[:], x1t[:], sqb[0][:], ALU.mult))
        p.step()
        p.do("vector", lambda E: E.tensor_tensor(lnstg[:], x1t[:], sqb[1][:], ALU.add))
        p.step()
        p.dma("sync", ln3_o[k * 128:(k + 1) * 128, :], lnstg[:])
    p.step()
    return p.build()



def build_c():
    nc = bass.Bass("TRN2", target_bir_lowering=False)
    CHQ = 1024
    ln3 = nc.dram_tensor("ln3", [D, T], BF16, kind="ExternalInput").ap()
    ipwx_d = nc.dram_tensor("ipw_x", [D, CHQ], F32, kind="ExternalInput").ap()
    ipwg_d = nc.dram_tensor("ipw_g", [D, CHQ], F32, kind="ExternalInput").ap()
    convw_d = nc.dram_tensor("convw", [CHQ, 4], F32, kind="ExternalInput").ap()
    convb_d = nc.dram_tensor("convb", [1, CHQ], F32, kind="ExternalInput").ap()
    u_o = nc.dram_tensor("u_s", [CHQ, T], F32, kind="ExternalOutput").ap()
    gate_o = nc.dram_tensor("gate_s", [CHQ, T], F32, kind="ExternalOutput").ap()
    xpscr = nc.dram_tensor("xpscr", [CHQ, T], F32, kind="Internal").ap()

    p = Prog(nc)
    p.init_negone()
    KT = D // 128
    MT = CHQ // 128
    ipwx = p.sbuf("ipwx", (128, KT * CHQ), BF16)
    ipwg = p.sbuf("ipwg", (128, KT * CHQ), BF16)
    lnblk = [p.sbuf(f"lnblk{i}", (128, KT * 512), BF16) for i in range(2)]
    evac = p.sbuf("evac", (128, 512), F32)
    evac2 = p.sbuf("evac2", (128, 512), F32)
    cw = p.sbuf("cw", (128, MT * 4), F32)
    cb = p.sbuf("cb", (128, MT), F32)
    xp = p.sbuf("xp", (128, 2048 + 4), F32)
    acc = p.sbuf("acc", (128, 2048), F32)
    ut = p.sbuf("ut", (128, 2048), F32)
    ps = p.psum("ps", (128, 512))
    ps2 = p.psum("ps2", (128, 512))

    p.dma("gpsimd", ipwx[:].rearrange("p (k c) -> p k c", k=KT),
          bass.AP(ipwx_d.tensor, 0, [[CHQ, 128], [CHQ * 128, KT], [1, CHQ]]))
    p.dma("gpsimd", ipwg[:].rearrange("p (k c) -> p k c", k=KT),
          bass.AP(ipwg_d.tensor, 0, [[CHQ, 128], [CHQ * 128, KT], [1, CHQ]]))
    for m in range(MT):
        p.dma("gpsimd", cw[:, m * 4:(m + 1) * 4],
              bass.AP(convw_d.tensor, m * 128 * 4, [[4, 128], [1, 4]]))
    p.dma("gpsimd", cb[:], bass.AP(convb_d.tensor, 0, [[1, 128], [128, MT]]),
          allow_slow_non_contiguous=True)
    p.dma("gpsimd", lnblk[0][:].rearrange("p (k n) -> p k n", k=KT),
          bass.AP(ln3.tensor, 0, [[T, 128], [T * 128, KT], [1, 512]]))
    p.step()
    NT = T // 512
    for tn in range(NT):
        cur = lnblk[tn % 2]
        p.step()
        if tn + 1 < NT:
            p.dma("gpsimd", lnblk[(tn + 1) % 2][:].rearrange("p (k n) -> p k n", k=KT),
                  bass.AP(ln3.tensor, (tn + 1) * 512, [[T, 128], [T * 128, KT], [1, 512]]))
        for m in range(MT):
            for k in range(KT):
                p.do("tensor", lambda E, m=m, k=k, cur=cur: E.matmul(
                    ps[:], ipwx[:, k * CHQ + m * 128: k * CHQ + m * 128 + 128],
                    cur[:, k * 512:(k + 1) * 512], start=(k == 0), stop=(k == KT - 1)))
            p.step()
            p.do("vector", lambda E: E.tensor_copy(evac[:], ps[:]))
            for k in range(KT):
                p.do("tensor", lambda E, m=m, k=k, cur=cur: E.matmul(
                    ps2[:], ipwg[:, k * CHQ + m * 128: k * CHQ + m * 128 + 128],
                    cur[:, k * 512:(k + 1) * 512], start=(k == 0), stop=(k == KT - 1)))
            p.step()
            p.dma("gpsimd", xpscr[m * 128:(m + 1) * 128, tn * 512:tn * 512 + 512], evac[:])
            p.do("vector", lambda E: E.tensor_copy(evac2[:], ps2[:]))
            p.step()
            p.dma("gpsimd", gate_o[m * 128:(m + 1) * 128, tn * 512:tn * 512 + 512], evac2[:])
    TC = 2048
    for m in range(MT):
        for c in range(T // TC):
            p.step()
            p.do("vector", lambda E: E.memset(xp[:, 0:4], 0.0))
            p.step()
            if c == 0:
                p.dma("gpsimd", xp[:, 3:3 + TC], xpscr[m * 128:(m + 1) * 128, 0:TC])
            else:
                p.dma("gpsimd", xp[:, 0:3 + TC], xpscr[m * 128:(m + 1) * 128, c * TC - 3:(c + 1) * TC])
            p.step()
            p.do("vector", lambda E, m=m: E.tensor_scalar(acc[:], xp[:, 0:TC], cw[:, m * 4:m * 4 + 1], None, ALU.mult))
            p.step()
            for kk in range(1, 4):
                p.do("vector", lambda E, m=m, kk=kk: E.scalar_tensor_tensor(
                    acc[:], xp[:, kk:kk + TC], cw[:, m * 4 + kk:m * 4 + kk + 1], acc[:], ALU.mult, ALU.add))
            p.step()
            p.do("scalar", lambda E, m=m: E.activation(ut[:], acc[:], AF.Silu, bias=cb[:, m:m + 1]))
            p.step()
            p.dma("sync", u_o[m * 128:(m + 1) * 128, c * TC:(c + 1) * TC], ut[:])
    p.step()
    return p.build()



def build_d(parts="all"):
    nc = bass.Bass("TRN2", target_bir_lowering=False)
    CHN = INTER
    u_d = nc.dram_tensor("u_t", [CHN, Q], F32, kind="ExternalInput").ap()
    xpw_d = nc.dram_tensor("xpw_t", [CHN, 160], F32, kind="ExternalInput").ap()
    dtw_d = nc.dram_tensor("dtw_t", [128, CHN], F32, kind="ExternalInput").ap()
    dtb_d = nc.dram_tensor("dtb", [1, CHN], F32, kind="ExternalInput").ap()
    delta_o = nc.dram_tensor("delta_t", [CHN, Q], F32, kind="ExternalOutput").ap()
    bc_o = nc.dram_tensor("bc_t", [32, Q], F32, kind="ExternalOutput").ap()
    dbg1 = nc.dram_tensor("dbg_dtlr", [128, Q], BF16, kind="ExternalOutput").ap()
    dbg2 = nc.dram_tensor("dbg_bct", [32, Q], F32, kind="ExternalOutput").ap()

    p = Prog(nc)
    p.init_negone()
    KT = CHN // 128
    MT = CHN // 128
    xpw = p.sbuf("xpw", (128, KT * 160), BF16)
    dtw = p.sbuf("dtw", (128, MT * 128), BF16)
    dtbc = p.sbuf("dtbc", (128, MT), F32)
    onec = p.const_tile("onec", 128, 1.0)
    ublk = p.sbuf("ublk", (128, 32 * 512), BF16)
    dtlr = p.sbuf("dtlr", (128, Q), BF16)
    bct = p.sbuf("bct", (32, Q), F32)
    ab = p.sbuf("ab", (128, 512), F32)
    eb = p.sbuf("eb", (128, 512), F32)
    eb2 = p.sbuf("eb2", (128, 512), F32)
    pb = p.sbuf("pb", (128, 512), F32)
    rb = p.sbuf("rb", (128, 512), F32)
    dstage = p.sbuf("dstage", (128, Q), F32)
    ps_a = p.psum("ps_a", (128, 512))
    ps_b = p.psum("ps_b", (32, 512))
    ps_d = p.psum("ps_d", (128, 512))

    p.dma("gpsimd", xpw[:].rearrange("p (k c) -> p k c", k=KT),
          bass.AP(xpw_d.tensor, 0, [[160, 128], [160 * 128, KT], [1, 160]]))
    p.dma("gpsimd", dtw[:], bass.AP(dtw_d.tensor, 0, [[CHN, 128], [1, CHN]]))
    p.dma("gpsimd", dtbc[:], bass.AP(dtb_d.tensor, 0, [[1, 128], [128, MT]]),
          allow_slow_non_contiguous=True)
    p.step()
    for tn in range(Q // 512):
        p.step()
        p.dma("gpsimd", ublk[:].rearrange("p (k n) -> p k n", k=KT),
              bass.AP(u_d.tensor, tn * 512, [[Q, 128], [Q * 128, KT], [1, 512]]))
        p.step()
        for k in range(KT):
            p.do("tensor", lambda E, k=k, tn=tn: E.matmul(
                ps_a[:], xpw[:, k * 160: k * 160 + 128], ublk[:, k * 512:(k + 1) * 512],
                start=(k == 0), stop=(k == KT - 1)))
            p.do("tensor", lambda E, k=k, tn=tn: E.matmul(
                ps_b[:], xpw[:, k * 160 + 128: k * 160 + 160], ublk[:, k * 512:(k + 1) * 512],
                start=(k == 0), stop=(k == KT - 1)))
        p.step()
        p.do("vector", lambda E, tn=tn: E.tensor_copy(dtlr[:, tn * 512:(tn + 1) * 512], ps_a[:]))
        p.do("vector", lambda E, tn=tn: E.tensor_copy(bct[:, tn * 512:(tn + 1) * 512], ps_b[:]))
    p.step()
    p.dma("sync", bc_o[:], bct[:])
    p.dma("gpsimd", dbg1[:], dtlr[:])
    p.dma("gpsimd", dbg2[:], bct[:])
    for m in (range(MT) if parts != "p1" else []):
        for tn in range(Q // 512):
            p.step()
            p.do("tensor", lambda E, m=m, tn=tn: E.matmul(
                ps_d[:], dtw[:, m * 128:(m + 1) * 128], dtlr[:, tn * 512:(tn + 1) * 512],
                start=True, stop=True))
            p.step()
            p.do("vector", lambda E: E.tensor_copy(pb[:], ps_d[:]))
            p.do("vector", lambda E, m=m, tn=tn: E.tensor_scalar(
                rb[:], ps_d[:], dtbc[:, m:m + 1], 0.0, ALU.add, ALU.max))
            if parts in ("p2b", "p2c", "all"):
                p.step()
                p.do("scalar", lambda E, m=m: E.activation(ab[:], pb[:], AF.Abs, bias=dtbc[:, m:m + 1]))
            if parts in ("p2b", "p2c", "all"):
                p.step()
                p.do("scalar", lambda E: E.activation(eb[:], ab[:], AF.Exp, scale=p.negone[:, 0:1]))
            if parts in ("p2c", "all"):
                p.step()
                p.do("scalar", lambda E: E.activation(eb2[:], eb[:], AF.Ln, bias=onec[:, 0:1]))
            p.step()
            if parts == "all":
                p.do("vector", lambda E, tn=tn: E.tensor_tensor(dstage[:, tn * 512:(tn + 1) * 512], rb[:], eb2[:], ALU.add))
            else:
                p.do("vector", lambda E, tn=tn: E.tensor_copy(dstage[:, tn * 512:(tn + 1) * 512], rb[:]))
        p.step()
        p.dma("gpsimd", delta_o[m * 128:(m + 1) * 128, :], dstage[:])
    p.step()
    return p.build()



def build_e(debug=False):
    nc = bass.Bass("TRN2", target_bir_lowering=False)
    CHQ = 1024
    TC = 512
    NCH = T // TC
    DT = CHQ // 128
    delta_d = nc.dram_tensor("delta_s", [CHQ, T], F32, kind="ExternalInput").ap()
    u_d = nc.dram_tensor("u_s", [CHQ, T], F32, kind="ExternalInput").ap()
    gate_d = nc.dram_tensor("gate_s", [CHQ, T], F32, kind="ExternalInput").ap()
    bct_d = nc.dram_tensor("bct", [32, T], F32, kind="ExternalInput").ap()
    alog_d = nc.dram_tensor("alog_s", [CHQ, NST], F32, kind="ExternalInput").ap()
    dpar_d = nc.dram_tensor("dpar_s", [1, CHQ], F32, kind="ExternalInput").ap()
    yg_o = nc.dram_tensor("yg_s", [CHQ, T], BF16, kind="ExternalOutput").ap()
    if debug:
        dbg_da = nc.dram_tensor("dbg_da", [128, 512], F32, kind="ExternalOutput").ap()
        dbg_dbu = nc.dram_tensor("dbg_dbu", [128, 512], F32, kind="ExternalOutput").ap()
        dbg_h = nc.dram_tensor("dbg_h", [128, 512], F32, kind="ExternalOutput").ap()
        dbg_b3 = nc.dram_tensor("dbg_b3", [128, 512], F32, kind="ExternalOutput").ap()
        dbg_y = nc.dram_tensor("dbg_y", [128, 512], F32, kind="ExternalOutput").ap()
        dbg_a = nc.dram_tensor("dbg_a", [128, 16], F32, kind="ExternalOutput").ap()
        dbg_w = nc.dram_tensor("dbg_w", [128, 512], F32, kind="ExternalOutput").ap()
        dbg_dc = nc.dram_tensor("dbg_dc", [128, 512], F32, kind="ExternalOutput").ap()
        dbg_uc = nc.dram_tensor("dbg_uc", [128, 512], F32, kind="ExternalOutput").ap()

    p = Prog(nc)
    p.init_negone()
    B3 = [p.sbuf(f"B3_{i}", (128, TC * NST), F32) for i in range(2)]
    C3 = [p.sbuf(f"C3_{i}", (128, TC * NST), F32) for i in range(2)]
    H = p.sbuf("H", (128, TC * NST), F32)
    dAb = p.sbuf("dAb", (128, TC), F32)
    dBu = p.sbuf("dBu", (128, TC), F32)
    wt = p.sbuf("wtl", (128, TC), F32)
    dc = [p.sbuf(f"dc{i}", (128, TC), F32) for i in range(2)]
    uc = [p.sbuf(f"uc{i}", (128, TC), F32) for i in range(2)]
    gc = [p.sbuf(f"gc{i}", (128, TC), F32) for i in range(2)]
    yb = p.sbuf("yb", (128, TC), F32)
    sg = p.sbuf("sg", (128, TC), F32)
    ygs = p.sbuf("ygs", (128, TC), BF16)
    stb = p.sbuf("stb", (128, DT * NST), F32)
    abuf = p.sbuf("abuf", (128, DT * NST), F32)
    dcol = p.sbuf("dcol", (128, DT), F32)
    junk = p.sbuf("junk", (128, 8), F32)

    def load_du(idx, buf):
        c, d = idx // DT, idx % DT
        p.dma("gpsimd", dc[buf][:], delta_d[d * 128:(d + 1) * 128, c * TC:(c + 1) * TC])
        p.dma("gpsimd", uc[buf][:], u_d[d * 128:(d + 1) * 128, c * TC:(c + 1) * TC])
        p.dma("gpsimd", gc[buf][:], gate_d[d * 128:(d + 1) * 128, c * TC:(c + 1) * TC])

    def load_bc(c, buf):
        p.dma("gpsimd", B3[buf][:].rearrange("p (n t) -> p n t", n=NST),
              bass.AP(bct_d.tensor, c * TC, [[0, 128], [T, NST], [1, TC]]))
        p.dma("gpsimd", C3[buf][:].rearrange("p (n t) -> p n t", n=NST),
              bass.AP(bct_d.tensor, c * TC + NST * T, [[0, 128], [T, NST], [1, TC]]))

    p.do("vector", lambda E: E.memset(stb[:], 0.0))
    for d in range(DT):
        p.dma("gpsimd", abuf[:, d * NST:(d + 1) * NST],
              bass.AP(alog_d.tensor, d * 128 * NST, [[NST, 128], [1, NST]]))
    p.dma("gpsimd", dcol[:], bass.AP(dpar_d.tensor, 0, [[1, 128], [128, DT]]),
          allow_slow_non_contiguous=True)
    load_bc(0, 0)
    load_du(0, 0)
    for _ in range(4):
        p.step()
        p.do("vector", lambda E: E.memset(junk[:], 0.0))
    p.step()
    p.do("scalar", lambda E: E.activation(abuf[:], abuf[:], AF.Exp))
    p.step()
    p.do("vector", lambda E: E.tensor_scalar_mul(abuf[:], abuf[:], -1.0))
    p.step()
    NIT = NCH * DT
    for idx in range(NIT):
        c, d = idx // DT, idx % DT
        buf = idx % 2
        cbuf = c % 2
        dci, uci, gci = dc[buf], uc[buf], gc[buf]
        p.step()
        if idx + 1 < NIT:
            load_du(idx + 1, (idx + 1) % 2)
        if d == 0 and c + 1 < NCH:
            load_bc(c + 1, (c + 1) % 2)
        p.do("vector", lambda E, dci=dci, uci=uci: E.tensor_tensor(wt[:], dci[:], uci[:], ALU.mult))
        for n in range(NST):
            p.step()
            p.do("scalar", lambda E, dci=dci, d=d, n=n: E.activation(
                dAb[:], dci[:], AF.Exp, scale=abuf[:, d * NST + n: d * NST + n + 1]))
            p.do("vector", lambda E, n=n, cbuf=cbuf: E.tensor_tensor(
                dBu[:], wt[:], B3[cbuf][:, n * TC:(n + 1) * TC], ALU.mult))
            p.step()
            p.do("vector", lambda E, d=d, n=n: E.tensor_tensor_scan(
                H[:, n * TC:(n + 1) * TC], dAb[:], dBu[:],
                stb[:, d * NST + n: d * NST + n + 1], ALU.mult, ALU.add))
        if debug and c == 0 and d == 0:
            p.step()
            p.dma("sync", dbg_w[:], wt[:])
            p.dma("sync", dbg_dc[:], dci[:])
            p.dma("sync", dbg_uc[:], uci[:])
            p.dma("sync", dbg_da[:], dAb[:])
            p.dma("sync", dbg_dbu[:], dBu[:])
            p.dma("sync", dbg_b3[:], B3[0][:, 0:TC])
            p.dma("sync", dbg_h[:], H[0:128, 0:TC])
        p.step()
        p.do("vector", lambda E, d=d: E.tensor_copy(
            stb[:, d * NST:(d + 1) * NST],
            bass.AP(H, TC - 1, [[TC * NST, 128], [TC, NST]])))
        p.step()
        p.do("vector", lambda E, cbuf=cbuf: E.tensor_tensor(H[:], H[:], C3[cbuf][:], ALU.mult))
        p.step()
        p.do("vector", lambda E: E.tensor_reduce(
            yb[:], bass.AP(H, 0, [[TC * NST, 128], [1, TC], [TC, NST]]), AX.X, ALU.add))
        p.step()
        p.do("scalar", lambda E, gci=gci: E.activation(sg[:], gci[:], AF.Silu))
        p.do("vector", lambda E, uci=uci, d=d: E.scalar_tensor_tensor(
            yb[:], uci[:], dcol[:, d:d + 1], yb[:], ALU.mult, ALU.add))
        if debug and c == 0 and d == 0:
            p.step()
            p.dma("sync", dbg_y[:], yb[:])
            p.dma("sync", dbg_a[:], abuf[:, 0:16])
        p.step()
        p.do("vector", lambda E: E.tensor_tensor(ygs[:], yb[:], sg[:], ALU.mult))
        p.step()
        p.dma("sync", yg_o[d * 128:(d + 1) * 128, c * TC:(c + 1) * TC], ygs[:])
    p.step()
    return p.build()




def build_f():
    nc = bass.Bass("TRN2", target_bir_lowering=False)
    NE = 8
    FI = 1024
    KY = INTER // 128
    KT = D // 128
    KH = (NE * FI + FI) // 128
    out2_d = nc.dram_tensor("out2q", [D, Q], F32, kind="ExternalInput").ap()
    yg_d = nc.dram_tensor("yg_t", [INTER, Q], BF16, kind="ExternalInput").ap()
    opw_d = nc.dram_tensor("opw_t", [INTER, D], F32, kind="ExternalInput").ap()
    rw1_d = nc.dram_tensor("rw1", [D, 8], F32, kind="ExternalInput").ap()
    gup_d = nc.dram_tensor("gup", [NE * D, 2 * FI], F32, kind="ExternalInput").ap()
    shgu_d = nc.dram_tensor("shgu", [D, 2 * FI], F32, kind="ExternalInput").ap()
    dws_d = nc.dram_tensor("dwstk", [KH * 128, D], F32, kind="ExternalInput").ap()
    ident_d = nc.dram_tensor("ident", [128, 128], F32, kind="ExternalInput").ap()
    fin_o = nc.dram_tensor("final_q", [D, Q], F32, kind="ExternalOutput").ap()
    x3scr = nc.dram_tensor("x3scr", [D, Q], F32, kind="Internal").ap()
    rowscr = nc.dram_tensor("rowscr", [4, Q], F32, kind="Internal").ap()
    wscr = nc.dram_tensor("wscr", [8, Q], F32, kind="Internal").ap()
    hscr = nc.dram_tensor("hscr", [KH * 128, Q], BF16, kind="Internal").ap()

    p = Prog(nc)
    p.init_negone()
    identf = p.sbuf("identf", (128, 128), F32)
    identf8 = p.sbuf("identf8", (8, 8), F32)
    rw1f = p.sbuf("rw1f", (128, KT * 8), F32)
    onescol = p.sbuf("onescol", (128, 1), F32)
    epsc = p.const_tile("epsc", 1, EPS_LN)
    rows = make_ln_rows(p, "ln", Q)
    accs = p.sbuf("accs", (1, Q), F32)
    accq = p.sbuf("accq", (1, Q), F32)
    rawacc = p.sbuf("rawacc", (8, Q), F32)
    rstdb = p.sbuf("rstdb", (128, Q), F32)
    shiftb = p.sbuf("shiftb", (128, Q), F32)
    scol = p.sbuf("scol", (128, KT), F32)
    shcol = p.sbuf("shcol", (128, KT), F32)
    rwsb = p.sbuf("rwsb", (128, 8), F32)
    wtok = p.sbuf("wtok", (128, KT * 8), F32)
    wfm = p.sbuf("wfm", (8, Q), F32)
    lgb = p.sbuf("lgb", (128, 8), F32)
    eqb = p.sbuf("eqb", (128, 8), F32)
    sigb = p.sbuf("sigb", (128, 8), F32)
    m12 = p.sbuf("m12", (128, 8), F32)
    junk = p.sbuf("junk", (128, 8), F32)
    sq = p.sbuf("sqf", (128, 512), F32)
    srow = p.sbuf("srow", (1, 512), F32)
    ps_mm, cm_psmm = p.psum_cm("ps_mm", (128, 512))
    ps_st, cm_psst = p.psum_cm("ps_st", (8, Q))
    ps_misc, cm_psmisc = p.psum_cm("ps_misc", (128, 128))

    _yg0, cm_yg0 = p.sbuf_cm("ygblk0", (128, KY * 512), BF16)
    _yg1, cm_yg1 = p.sbuf_cm("ygblk1", (128, KY * 512), BF16)
    ygblk = [_yg0, _yg1]
    _op0, cm_op0 = p.sbuf_cm("opwb0", (128, KY * 128), BF16)
    _op1, cm_op1 = p.sbuf_cm("opwb1", (128, KY * 128), BF16)
    opwb = [_op0, _op1]
    _o2a, cm_o2a = p.sbuf_cm("o2t0", (128, 512), F32)
    _o2b, cm_o2b = p.sbuf_cm("o2t1", (128, 512), F32)
    o2t = [_o2a, _o2b]
    _o3a, cm_o3a = p.sbuf_cm("o3t0", (128, 512), F32)
    _o3b, cm_o3b = p.sbuf_cm("o3t1", (128, 512), F32)
    o3t = [_o3a, _o3b]

    def ld_yg(tn, b):
        p.dma("gpsimd", ygblk[b][:].rearrange("p (k n) -> p k n", k=KY),
              bass.AP(yg_d.tensor, tn * 512, [[Q, 128], [Q * 128, KY], [1, 512]]))

    def ld_opw(o, b):
        p.dma("gpsimd", opwb[b][:].rearrange("p (k c) -> p k c", k=KY),
              bass.AP(opw_d.tensor, o * 128, [[D, 128], [D * 128, KY], [1, 128]]))

    def ld_o2(o, tn, b):
        p.dma("gpsimd", o2t[b][:], bass.AP(out2_d.tensor, o * 128 * Q + tn * 512,
                                           [[Q, 128], [1, 512]]))

    p.dma("sync", identf[:], ident_d[:])
    p.dma("sync", identf8[:], ident_d[0:8, 0:8])
    p.dma("sync", rw1f[:].rearrange("p (k c) -> p k c", k=KT),
          bass.AP(rw1_d.tensor, 0, [[8, 128], [8 * 128, KT], [1, 8]]))
    p.do("vector", lambda E: E.memset(onescol[:], 1.0))
    p.do("vector", lambda E: E.memset(accs[:], 0.0))
    p.do("vector", lambda E: E.memset(accq[:], 0.0))
    p.do("vector", lambda E: E.memset(rawacc[:], 0.0))
    ld_yg(0, 0)
    ld_opw(0, 0)
    ld_o2(0, 0, 0)
    for _ in range(3):
        p.step()
        p.do("vector", lambda E: E.memset(junk[:], 0.0))
    p.step()
    for k in range(KT):
        p.do("tensor", lambda E, k=k: E.matmul(ps_misc[0:8, 0:1], rw1f[:, k * 8:(k + 1) * 8],
                                               onescol[:, 0:1], start=(k == 0), stop=(k == KT - 1)))
    p.step()
    p.do("vector", lambda E: E.tensor_copy(m12[0:8, 0:1], ps_misc[0:8, 0:1]))
    it = 0
    for tn in range(Q // 512):
        for o in range(KT):
            b = it % 2
            p.step()
            if o + 1 < KT:
                ld_opw(o + 1, (it + 1) % 2)
                ld_o2(o + 1, tn, (it + 1) % 2)
            elif tn + 1 < Q // 512:
                ld_yg(tn + 1, (tn + 1) % 2)
                ld_opw(0, (it + 1) % 2)
                ld_o2(0, tn + 1, (it + 1) % 2)
            for k in range(KY):
                p.do("tensor", lambda E, k=k, b=b, tn=tn: E.matmul(
                    ps_mm[:], opwb[b][:, k * 128:(k + 1) * 128],
                    ygblk[tn % 2][:, k * 512:(k + 1) * 512], start=(k == 0), stop=(k == KY - 1)))
            p.step()
            p.do("vector", lambda E, b=b: E.tensor_tensor(o3t[b][:], ps_mm[:], o2t[b][:], ALU.add))
            p.step()
            p.do("scalar", lambda E, b=b: E.activation(sq[:], o3t[b][:], AF.Square))
            p.dma("gpsimd", bass.AP(x3scr.tensor, o * 128 * Q + tn * 512, [[Q, 128], [1, 512]]), o3t[b][:])
            p.do("tensor", lambda E, o=o, b=b: E.matmul(
                ps_st[0:1, 0:512], onescol[:, 0:1], o3t[b][:],
                start=True, stop=True))
            p.step()
            p.do("vector", lambda E, tn=tn: E.tensor_tensor(
                accs[0:1, tn * 512:(tn + 1) * 512], accs[0:1, tn * 512:(tn + 1) * 512],
                ps_st[0:1, 0:512], ALU.add))
            p.do("tensor", lambda E: E.matmul(ps_st[0:1, 512:1024], onescol[:, 0:1], sq[:],
                                              start=True, stop=True))
            p.step()
            p.do("vector", lambda E, tn=tn: E.tensor_tensor(
                accq[0:1, tn * 512:(tn + 1) * 512], accq[0:1, tn * 512:(tn + 1) * 512],
                ps_st[0:1, 512:1024], ALU.add))
            p.do("tensor", lambda E, o=o, b=b: E.matmul(
                ps_st[0:8, 1024:1536], rw1f[:, o * 8:(o + 1) * 8],
                o3t[b][:], start=True, stop=True))
            p.step()
            p.do("vector", lambda E, tn=tn: E.tensor_tensor(
                rawacc[0:8, tn * 512:(tn + 1) * 512], rawacc[0:8, tn * 512:(tn + 1) * 512],
                ps_st[0:8, 1024:1536], ALU.add))
            it += 1
    for cmx in (cm_o3b, cm_o3a, cm_o2b, cm_o2a, cm_op1, cm_op0, cm_yg1, cm_yg0):
        p.free(cmx)
    emit_ln_finish(p, nc, accs, accq, Q, D, rows, epsc)
    p.step()
    p.dma("sync", rowscr[0:1, :], rows["rstd"][0:1, :])
    p.dma("sync", rowscr[1:2, :], rows["shift"][0:1, :])
    p.dma("sync", rowscr[3:4, 0:8], m12[0:8, 0:1], allow_slow_non_contiguous=True)
    p.step()
    p.dma("gpsimd", rstdb[:], bass.AP(rowscr.tensor, 0, [[0, 128], [1, Q]]))
    p.dma("gpsimd", shiftb[:], bass.AP(rowscr.tensor, Q, [[0, 128], [1, Q]]))
    p.dma("gpsimd", scol[:], bass.AP(rowscr.tensor, 0, [[1, 128], [128, KT]]), allow_slow_non_contiguous=True)
    p.dma("gpsimd", shcol[:], bass.AP(rowscr.tensor, Q, [[1, 128], [128, KT]]), allow_slow_non_contiguous=True)
    p.dma("gpsimd", rwsb[:], bass.AP(rowscr.tensor, 3 * Q, [[0, 128], [1, 8]]))
    for _ in range(2):
        p.step()
        p.do("vector", lambda E: E.memset(junk[:], 0.0))
    ln4b, cm_ln4 = p.sbuf_cm("ln4b", (128, KT * Q), BF16)
    _x3a, cm_x3a = p.sbuf_cm("x3t0", (128, Q), F32)
    _x3b, cm_x3b = p.sbuf_cm("x3t1", (128, Q), F32)
    x3t = [_x3a, _x3b]
    ott, cm_ott = p.sbuf_cm("ott", (128, Q), F32)
    p.step()
    p.dma("gpsimd", x3t[0][:], x3scr[0:128, :])
    p.step()
    p.do("vector", lambda E: E.memset(junk[:], 0.0))
    for k in range(KT):
        p.step()
        if k + 1 < KT:
            p.dma("gpsimd", x3t[(k + 1) % 2][:], x3scr[(k + 1) * 128:(k + 2) * 128, :])
        p.do("vector", lambda E, k=k: E.tensor_tensor(ott[:], x3t[k % 2][:], rstdb[:], ALU.mult))
        p.step()
        p.do("vector", lambda E, k=k: E.tensor_tensor(ln4b[:, k * Q:(k + 1) * Q], ott[:], shiftb[:], ALU.add))
    p.free(cm_ott); p.free(cm_x3b); p.free(cm_x3a)
    for tt in range(KT):
        t0 = tt * 128
        p.step()
        p.do("tensor", lambda E, t0=t0: E.matmul(ps_misc[0:128, 0:8], rawacc[0:8, t0:t0 + 128],
                                                 identf8[:], start=True, stop=True))
        p.step()
        p.do("vector", lambda E, tt=tt: E.tensor_scalar(lgb[:], ps_misc[0:128, 0:8], scol[:, tt:tt + 1], None, ALU.mult))
        p.step()
        p.do("vector", lambda E, tt=tt: E.scalar_tensor_tensor(lgb[:], rwsb[:], shcol[:, tt:tt + 1], lgb[:], ALU.mult, ALU.add))
        p.step()
        p.do("vector", lambda E: E.reduce_max(m12[:, 0:1], lgb[:], AX.X))
        p.step()
        p.do("vector", lambda E: E.tensor_scalar(eqb[:], lgb[:], m12[:, 0:1], None, ALU.is_equal))
        p.step()
        p.do("vector", lambda E: E.scalar_tensor_tensor(eqb[:], eqb[:], -1e30, lgb[:], ALU.mult, ALU.add))
        p.step()
        p.do("vector", lambda E: E.reduce_max(m12[:, 1:2], eqb[:], AX.X))
        p.step()
        p.do("scalar", lambda E: E.activation(sigb[:], lgb[:], AF.Sigmoid))
        p.do("vector", lambda E: E.tensor_scalar(eqb[:], lgb[:], m12[:, 1:2], None, ALU.is_ge), wait_self=True)
        p.step()
        p.do("vector", lambda E, tt=tt: E.tensor_tensor(wtok[:, tt * 8:(tt + 1) * 8], sigb[:], eqb[:], ALU.mult))
        p.step()
        p.do("tensor", lambda E, tt=tt: E.matmul(ps_misc[0:8, 0:128], wtok[:, tt * 8:(tt + 1) * 8],
                                                 identf[:], start=True, stop=True))
        p.step()
        p.do("vector", lambda E, t0=t0: E.tensor_copy(wfm[:, t0:t0 + 128], ps_misc[0:8, 0:128]))
    p.step()
    p.dma("gpsimd", wscr[:], wfm[:])
    for _ in range(2):
        p.step()
        p.do("vector", lambda E: E.memset(junk[:], 0.0))
    p.free(cm_psmisc); p.free(cm_psst); p.free(cm_psmm)
    psA, cm_psA = p.psum_cm("psA", (128, 2048))
    psB, cm_psB = p.psum_cm("psB", (128, 2048))
    _gw0, cm_gw0 = p.sbuf_cm("gwb0", (128, KT * 128), BF16)
    _gw1, cm_gw1 = p.sbuf_cm("gwb1", (128, KT * 128), BF16)
    gwb = [_gw0, _gw1]
    _uw0, cm_uw0 = p.sbuf_cm("uwb0", (128, KT * 128), BF16)
    _uw1, cm_uw1 = p.sbuf_cm("uwb1", (128, KT * 128), BF16)
    uwb = [_uw0, _uw1]
    _pb0, cm_pb0 = p.sbuf_cm("pbc0", (128, Q), F32)
    _pb1, cm_pb1 = p.sbuf_cm("pbc1", (128, Q), F32)
    pbc = [_pb0, _pb1]
    gst, cm_gst = p.sbuf_cm("gst", (128, Q), F32)
    ust, cm_ust = p.sbuf_cm("ust", (128, Q), F32)
    hst, cm_hst = p.sbuf_cm("hst", (128, Q), BF16)

    def ld_guw(e, fp, b):
        base = shgu_d if e == NE else gup_d
        off0 = (0 if e == NE else e * D * 2 * FI)
        p.dma("gpsimd", gwb[b][:].rearrange("p (k c) -> p k c", k=KT),
              bass.AP(base.tensor, off0 + fp * 128, [[2 * FI, 128], [2 * FI * 128, KT], [1, 128]]))
        p.dma("gpsimd", uwb[b][:].rearrange("p (k c) -> p k c", k=KT),
              bass.AP(base.tensor, off0 + FI + fp * 128, [[2 * FI, 128], [2 * FI * 128, KT], [1, 128]]))

    def ld_pbc(e, b):
        p.dma("gpsimd", pbc[b][:], bass.AP(wscr.tensor, e * Q, [[0, 128], [1, Q]]))

    p.step()
    ld_guw(0, 0, 0)
    ld_pbc(0, 0)
    for _ in range(2):
        p.step()
        p.do("vector", lambda E: E.memset(junk[:], 0.0))
    FP = FI // 128
    jt = 0
    for e in range(NE + 1):
        for fp in range(FP):
            b = jt % 2
            p.step()
            if fp + 1 < FP:
                ld_guw(e, fp + 1, (jt + 1) % 2)
            elif e + 1 < NE + 1:
                ld_guw(e + 1, 0, (jt + 1) % 2)
                if e + 1 < NE:
                    ld_pbc(e + 1, (e + 1) % 2)
            for nn in range(4):
                for k in range(KT):
                    p.do("tensor", lambda E, k=k, nn=nn, b=b: E.matmul(
                        psA[:, nn * 512:(nn + 1) * 512], gwb[b][:, k * 128:(k + 1) * 128],
                        ln4b[:, k * Q + nn * 512: k * Q + nn * 512 + 512],
                        start=(k == 0), stop=(k == KT - 1)))
                for k in range(KT):
                    p.do("tensor", lambda E, k=k, nn=nn, b=b: E.matmul(
                        psB[:, nn * 512:(nn + 1) * 512], uwb[b][:, k * 128:(k + 1) * 128],
                        ln4b[:, k * Q + nn * 512: k * Q + nn * 512 + 512],
                        start=(k == 0), stop=(k == KT - 1)))
            p.step()
            if e < NE:
                p.do("vector", lambda E, e=e: E.tensor_tensor(gst[:], psA[:], pbc[e % 2][:], ALU.mult))
                p.do("vector", lambda E, e=e: E.tensor_tensor(ust[:], psB[:], pbc[e % 2][:], ALU.mult))
            else:
                p.do("vector", lambda E: E.tensor_copy(gst[:], psA[:]))
                p.do("vector", lambda E: E.tensor_copy(ust[:], psB[:]))
            p.step()
            p.do("scalar", lambda E: E.activation(gst[:], gst[:], AF.Silu))
            p.step()
            p.do("vector", lambda E: E.tensor_tensor(hst[:], gst[:], ust[:], ALU.mult))
            p.step()
            row = (FI if e < NE else 0) + (e if e < NE else 0) * FI + fp * 128
            p.dma("gpsimd", hscr[row:row + 128, :], hst[:])
            jt += 1
    for cmx in (cm_hst, cm_ust, cm_gst, cm_pb1, cm_pb0, cm_uw1, cm_uw0, cm_gw1, cm_gw0):
        p.free(cmx)
    p.free(cm_ln4)
    p.free(cm_psB); p.free(cm_psA)
    psD, cm_psD = p.psum_cm("psD", (128, 512))
    hblk, cm_hblk = p.sbuf_cm("hblk", (128, KH * 512), BF16)
    _dw0, cm_dw0 = p.sbuf_cm("dwb0", (128, KH * 128), BF16)
    _dw1, cm_dw1 = p.sbuf_cm("dwb1", (128, KH * 128), BF16)
    dwb = [_dw0, _dw1]
    _of0, cm_of0 = p.sbuf_cm("o2f0", (128, 512), F32)
    _of1, cm_of1 = p.sbuf_cm("o2f1", (128, 512), F32)
    o2f = [_of0, _of1]
    _fs0, cm_fs0 = p.sbuf_cm("fst0", (128, 512), F32)
    _fs1, cm_fs1 = p.sbuf_cm("fst1", (128, 512), F32)
    fst = [_fs0, _fs1]

    def ld_dw(o, b):
        p.dma("gpsimd", dwb[b][:].rearrange("p (k c) -> p k c", k=KH),
              bass.AP(dws_d.tensor, o * 128, [[D, 128], [D * 128, KH], [1, 128]]))

    def ld_x3(o, tn, b):
        p.dma("gpsimd", o2f[b][:], bass.AP(x3scr.tensor, o * 128 * Q + tn * 512,
                                           [[Q, 128], [1, 512]]))

    for tn in range(Q // 512):
        p.step()
        p.dma("gpsimd", hblk[:].rearrange("p (k n) -> p k n", k=KH),
              bass.AP(hscr.tensor, tn * 512, [[Q, 128], [Q * 128, KH], [1, 512]]))
        if tn == 0:
            ld_dw(0, 0)
            ld_x3(0, 0, 0)
        for _ in range(2):
            p.step()
            p.do("vector", lambda E: E.memset(junk[:], 0.0))
        for o in range(KT):
            b = o % 2
            p.step()
            if o + 1 < KT:
                ld_dw(o + 1, (o + 1) % 2)
                ld_x3(o + 1, tn, (o + 1) % 2)
            elif tn + 1 < Q // 512:
                ld_dw(0, (o + 1) % 2)
                ld_x3(0, tn + 1, (o + 1) % 2)
            for k in range(KH):
                p.do("tensor", lambda E, k=k, b=b: E.matmul(
                    psD[:], dwb[b][:, k * 128:(k + 1) * 128],
                    hblk[:, k * 512:(k + 1) * 512], start=(k == 0), stop=(k == KH - 1)))
            p.step()
            p.do("vector", lambda E, b=b: E.tensor_tensor(fst[b][:], psD[:], o2f[b][:], ALU.add))
            p.step()
            p.dma("gpsimd", bass.AP(fin_o.tensor, o * 128 * Q + tn * 512, [[Q, 128], [1, 512]]),
                  fst[b][:])
    p.step()
    for cmx in (cm_fs1, cm_fs0, cm_of1, cm_of0, cm_dw1, cm_dw0, cm_hblk, cm_psD):
        p.free(cmx)
    return p.build()




def _bf(x):
    return np.ascontiguousarray(x.astype(BF16NP) if x.dtype != BF16NP else x)


def _f32c(x):
    return np.ascontiguousarray(x, dtype=np.float32)


def run_a0(x):
    nc = build_a0()
    in_maps = []
    for r in range(NCORE):
        b, q = r // 4, r % 4
        xq = _f32c(x[b, q * Q:(q + 1) * Q, :].T)
        in_maps.append({"xq": xq})
    res = run_bass_kernel_spmd(nc, in_maps, core_ids=list(range(NCORE)))
    ln1 = []
    for b in range(B):
        ln1.append(np.concatenate([res.results[4 * b + q]["lnq"] for q in range(4)], axis=1))
    return ln1


def run_a(ln1, Wq, bq, Wk, bk, Wv, bv):
    nc = build_a()
    maskT = np.triu(np.ones((CH, CH), np.float32))
    in_maps = []
    for r in range(NCORE):
        b, hq = r // 4, r % 4
        sl = slice(hq * 512, (hq + 1) * 512)
        in_maps.append({
            "ln1": ln1[b],
            "wq_s": _f32c(Wq[:, sl]), "wk_s": _f32c(Wk[:, sl]), "wv_s": _f32c(Wv[:, sl]),
            "bq_s": _f32c(bq[sl][None, :]), "bk_s": _f32c(bk[sl][None, :]),
            "bv_s": _f32c(bv[sl][None, :]),
            "maskT": maskT, "ident": np.eye(128, dtype=np.float32),
        })
    res = run_bass_kernel_spmd(nc, in_maps, core_ids=list(range(NCORE)))
    attn = []
    for b in range(B):
        attn.append(np.concatenate([res.results[4 * b + h]["attn_s"] for h in range(4)], axis=0))
    return attn


def run_b(x, attn, Wo, bo, W0, rA, rB, rw0):
    nc = build_b()
    ident = np.eye(128, dtype=np.float32)
    w0t = _f32c(W0.T)
    rw0t = _f32c(rw0.T)
    a_allt = _f32c(rA.transpose(2, 0, 1).reshape(D, 128))
    b_stack = _f32c(rB.transpose(0, 2, 1).reshape(128, D))
    in_maps = []
    for r in range(NCORE):
        b, q = r // 4, r % 4
        in_maps.append({
            "xq": _f32c(x[b, q * Q:(q + 1) * Q, :].T),
            "attn_q": np.ascontiguousarray(attn[b][:, q * Q:(q + 1) * Q]),
            "wo": _f32c(Wo), "bo": _f32c(bo[None, :]), "w0t": w0t, "rw0t": rw0t,
            "a_allt": a_allt, "b_stack": b_stack, "ident": ident,
        })
    res = run_bass_kernel_spmd(nc, in_maps, core_ids=list(range(NCORE)))
    out2 = [res.results[r]["out2q"] for r in range(NCORE)]
    ln3 = [res.results[r]["ln3q"] for r in range(NCORE)]
    return out2, ln3


def run_c(ln3, in_proj_w, conv_w, conv_b):
    nc = build_c()
    ipwT = np.ascontiguousarray(in_proj_w.T)
    cw = conv_w[:, 0, :]
    in_maps = []
    for r in range(NCORE):
        b, cq = r // 4, r % 4
        sl = slice(cq * 1024, (cq + 1) * 1024)
        in_maps.append({
            "ln3": ln3[b],
            "ipw_x": _f32c(ipwT[:, sl]),
            "ipw_g": _f32c(ipwT[:, INTER + cq * 1024: INTER + (cq + 1) * 1024]),
            "convw": _f32c(cw[sl]),
            "convb": _f32c(conv_b[sl][None, :]),
        })
    res = run_bass_kernel_spmd(nc, in_maps, core_ids=list(range(NCORE)))
    u = [np.concatenate([res.results[4 * b + c]["u_s"] for c in range(4)], axis=0) for b in range(B)]
    gate = [np.concatenate([res.results[4 * b + c]["gate_s"] for c in range(4)], axis=0) for b in range(B)]
    return u, gate


def run_d(u, x_proj_w, dt_proj_w, dt_proj_b, parts="all"):
    nc = build_d(parts)
    xpwT = _f32c(x_proj_w.T)
    dtwT = _f32c(dt_proj_w.T)
    in_maps = []
    for r in range(NCORE):
        b, tq = r // 4, r % 4
        in_maps.append({
            "u_t": np.ascontiguousarray(u[b][:, tq * Q:(tq + 1) * Q]),
            "xpw_t": xpwT, "dtw_t": dtwT,
            "dtb": _f32c(dt_proj_b[None, :]),
        })
    res = run_bass_kernel_spmd(nc, in_maps, core_ids=list(range(NCORE)))
    global _last_res
    _last_res = res
    delta = [np.concatenate([res.results[4 * b + t]["delta_t"] for t in range(4)], axis=1) for b in range(B)]
    bc = [np.concatenate([res.results[4 * b + t]["bc_t"] for t in range(4)], axis=1) for b in range(B)]
    return delta, bc


def run_e(delta, u, gate, bc, A_log, D_param, debug=False):
    nc = build_e(debug)
    in_maps = []
    for r in range(NCORE):
        b, cq = r // 4, r % 4
        sl = slice(cq * 1024, (cq + 1) * 1024)
        in_maps.append({
            "delta_s": np.ascontiguousarray(delta[b][sl]),
            "u_s": np.ascontiguousarray(u[b][sl]),
            "gate_s": np.ascontiguousarray(gate[b][sl]),
            "bct": _f32c(bc[b]),
            "alog_s": _f32c(A_log[sl]),
            "dpar_s": _f32c(D_param[sl][None, :]),
        })
    res = run_bass_kernel_spmd(nc, in_maps, core_ids=list(range(NCORE)))
    global _last_res
    _last_res = res
    yg = [np.concatenate([res.results[4 * b + c]["yg_s"] for c in range(4)], axis=0) for b in range(B)]
    return yg


def run_f(out2cores, yg, out_proj_w, rw1, gate_up_w, sgw, suw, sdw, down_w):
    nc = build_f()
    opwT = _f32c(out_proj_w.T)
    gup = _f32c(gate_up_w.reshape(8 * D, 2048))
    shgu = _f32c(np.concatenate([sgw, suw], axis=1))
    dwstk = _f32c(np.concatenate([sdw, down_w.reshape(8 * 1024, D)], axis=0))
    ident = np.eye(128, dtype=np.float32)
    in_maps = []
    for r in range(NCORE):
        b, tq = r // 4, r % 4
        in_maps.append({
            "out2q": out2cores[r],
            "yg_t": np.ascontiguousarray(yg[b][:, tq * Q:(tq + 1) * Q]),
            "opw_t": opwT, "rw1": _f32c(rw1), "gup": gup, "shgu": shgu,
            "dwstk": dwstk, "ident": ident,
        })
    res = run_bass_kernel_spmd(nc, in_maps, core_ids=list(range(NCORE)))
    return [res.results[r]["final_q"] for r in range(NCORE)]


def kernel(**inputs):
    x = np.asarray(inputs["L0_attn_x"], dtype=np.float32)
    ln1 = run_a0(x)
    attn = run_a(ln1, inputs["L0_attn_Wq"], inputs["L0_attn_bq"], inputs["L0_attn_Wk"],
                 inputs["L0_attn_bk"], inputs["L0_attn_Wv"], inputs["L0_attn_bv"])
    out2, ln3 = run_b(x, attn, inputs["L0_attn_Wo"], inputs["L0_attn_bo"],
                      inputs["L0_ffn_weight"], inputs["L0_ffn_lora_A"],
                      inputs["L0_ffn_lora_B"], inputs["L0_ffn_router_w"])
    ln3f = [np.ascontiguousarray(np.concatenate([ln3[4 * b + q] for q in range(4)], axis=1))
            for b in range(B)]
    u, gate = run_c(ln3f, inputs["L1_attn_in_proj_w"], inputs["L1_attn_conv_w"],
                    inputs["L1_attn_conv_b"])
    delta, bc = run_d(u, inputs["L1_attn_x_proj_w"], inputs["L1_attn_dt_proj_w"],
                      inputs["L1_attn_dt_proj_b"])
    yg = run_e(delta, u, gate, bc, inputs["L1_attn_A_log"], inputs["L1_attn_D_param"])
    fin = run_f(out2, yg, inputs["L1_attn_out_proj_w"], inputs["L1_ffn_router_w"],
                inputs["L1_ffn_gate_up_w"], inputs["L1_ffn_shared_gate_w"],
                inputs["L1_ffn_shared_up_w"], inputs["L1_ffn_shared_down_w"],
                inputs["L1_ffn_down_w"])
    out = np.empty((B, T, D), dtype=np.float32)
    for b in range(B):
        full = np.concatenate([fin[4 * b + q] for q in range(4)], axis=1)
        out[b] = full.T
    return out
